# version 02
# baseline (speedup 1.0000x reference)
import contextlib
import numpy as np
import ml_dtypes
import concourse.bass as bass
import concourse.mybir as mybir
from concourse.bass_utils import run_bass_kernel_spmd

F32 = mybir.dt.float32
BF16 = mybir.dt.bfloat16
AF = mybir.ActivationFunctionType
ALU = mybir.AluOpType
AX = mybir.AxisListType

NCORES = 8
D = 2048
DK = D // 128
HID = 5504
HC = HID // 128
HID_REAL = 5461
NTOK = 2048
FTC = 1024
FTT = FTC // 128
ATC = 512
ATT = ATC // 128
NG = ATC // 8
H = 16
EPS = 1e-5
CNT_REAL = float(HID_REAL * D)
SEGS = [11, 11, 11, 10]
assert sum(SEGS) == HC

ENGS = ("sync", "gpsimd", "tensor", "vector", "scalar")
_PHASES = ("ffn1", "attn", "ffn2")


class Plan:

    def __init__(self):
        self.ops = {e: [] for e in ENGS}
        self.counts = {}
        self.waited = {e: {} for e in ENGS}

    def sem(self, name):
        self.counts.setdefault(name, 0)

    def op(self, eng, fn, waits=(), inc=None, inc_by=1):
        waits = list(waits)
        if inc is not None and inc_by == 16:
            waits.append((inc, self.counts[inc]))
        pruned = []
        w = self.waited[eng]
        for s, v in waits:
            if v > w.get(s, 0):
                w[s] = v
                pruned.append((s, v))
        if inc is not None:
            self.counts[inc] += inc_by
        self.ops[eng].append((fn, tuple(pruned), inc, inc_by))
        return self.counts[inc] if inc is not None else None

    def cur(self, name):
        return self.counts[name]

    def emit(self, eng_obj, eng_name, sems):
        for fn, waits, inc, inc_by in self.ops[eng_name]:
            for s, v in waits:
                eng_obj.wait_ge(sems[s], v)
            r = fn(eng_obj)
            if inc is not None:
                r.then_inc(sems[inc], inc_by)


def build_nc(mode="p1"):
    nc = bass.Bass("TRN2", target_bir_lowering=False, debug=False)

    d_x = nc.dram_tensor("x_in", [NTOK, D], F32, kind="ExternalInput").ap()
    d_y = nc.dram_tensor("y_out", [NTOK, D], F32, kind="ExternalOutput").ap()
    d_wq = nc.dram_tensor("wqT", [D, D], F32, kind="ExternalInput").ap()
    d_wk = nc.dram_tensor("wkT", [D, D], F32, kind="ExternalInput").ap()
    d_wv = nc.dram_tensor("wvT", [D, D], F32, kind="ExternalInput").ap()
    d_wo = nc.dram_tensor("woT", [D, D], F32, kind="ExternalInput").ap()
    d_w1 = nc.dram_tensor("w1T", [D, HID], F32, kind="ExternalInput").ap()
    d_w3 = nc.dram_tensor("w3T", [D, HID], F32, kind="ExternalInput").ap()
    d_w2 = nc.dram_tensor("w2T", [HID, D], F32, kind="ExternalInput").ap()
    d_bqkv = nc.dram_tensor("bqkv", [128, 48], F32, kind="ExternalInput").ap()
    d_bo = nc.dram_tensor("bo_row", [1, D], BF16, kind="ExternalInput").ap()
    d_ones1b = nc.dram_tensor("ones1b", [1, 128], BF16, kind="ExternalInput").ap()
    d_ident = nc.dram_tensor("ident", [128, 128], BF16, kind="ExternalInput").ap()
    d_mask = nc.dram_tensor("maskB", [128, 128], F32, kind="ExternalInput").ap()
    d_cst = nc.dram_tensor("cst", [128, 4], F32, kind="ExternalInput").ap()
    d_ones1 = nc.dram_tensor("ones1", [1, 128], F32, kind="ExternalInput").ap()
    d_ones128 = nc.dram_tensor("ones128", [128, 1], F32, kind="ExternalInput").ap()
    d_xm1 = nc.dram_tensor("xm1", [NTOK, D], F32, kind="Internal").ap()
    d_xm2 = nc.dram_tensor("xm2", [NTOK, D], F32, kind="Internal").ap()
    if mode == "p1":
        d_x1o = nc.dram_tensor("x1_out", [NTOK, D], F32, kind="ExternalOutput").ap()
        d_oo = nc.dram_tensor("o_out", [H, 128, NTOK], BF16, kind="ExternalOutput").ap()
    else:
        d_x1i = nc.dram_tensor("x1_in", [NTOK, D], F32, kind="ExternalInput").ap()
        d_oi = nc.dram_tensor("o_in", [8, 128, 2 * NTOK], BF16, kind="ExternalInput").ap()
    d_q1c = nc.dram_tensor("q1c", [HC, 128, D], BF16, kind="Internal").ap()
    d_q3c = nc.dram_tensor("q3c", [HC, 128, D], BF16, kind="Internal").ap()
    d_q2c = nc.dram_tensor("q2c", [HC, 128, D], BF16, kind="Internal").ap()

    v_w1 = d_w1.rearrange("(c p) f -> p c f", p=128)
    v_w3 = d_w3.rearrange("(c p) f -> p c f", p=128)
    v_wq = d_wq.rearrange("(c p) f -> p c f", p=128)
    v_wk = d_wk.rearrange("(c p) f -> p c f", p=128)
    v_wv = d_wv.rearrange("(c p) f -> p c f", p=128)
    v_wo = d_wo.rearrange("(c p) f -> p c f", p=128)
    v_x = d_x.rearrange("(t p) d -> p t d", p=128)
    v_y = d_y.rearrange("(t p) d -> p t d", p=128)
    v_xm1 = d_xm1.rearrange("(t p) d -> p t d", p=128)
    v_xm2 = d_xm2.rearrange("(t p) d -> p t d", p=128)
    fws = [d_w1.flatten().rearrange("(n p f) -> n p f", p=128, f=2048),
           d_w3.flatten().rearrange("(n p f) -> n p f", p=128, f=2048),
           d_w2.flatten().rearrange("(n p f) -> n p f", p=128, f=2048)]

    p = Plan()
    for s in ("s_x", "s_wa", "s_wb", "s_gp", "s_gp2", "s_pe", "s_ve", "s_se", "s_out"):
        p.sem(s)
    OP = p.op

    es = contextlib.ExitStack()
    with es:
        sbt = lambda name, shape, dt: es.enter_context(nc.sbuf_tensor(name, shape, dt))
        xch = sbt("xch", [128, FTT * D], F32)
        xln = sbt("xln", [128, DK * FTC], BF16)
        big = sbt("big", [128, 24576], BF16)
        oarn = sbt("oarn", [128, 8192], BF16)
        warn = sbt("warn", [128, 16384], BF16)
        abuf = sbt("abuf", [128, 4096], BF16)
        bo_row = sbt("bo_rows", [1, D], BF16)
        ones1b = sbt("ones1bs", [1, 128], BF16)
        stats = sbt("stats", [128, 64], F32)
        sbc = sbt("sbc", [128, 8], F32)
        prt = sbt("prt", [128, 144], F32)
        pr1 = sbt("pr1", [128, 4], F32)
        sv3 = sbt("sv3", [1, 16], F32)
        bqkv = sbt("bqkvs", [128, 48], F32)
        cst = sbt("csts", [128, 4], F32)
        ident = sbt("idents", [128, 128], BF16)
        maskB = sbt("maskBs", [128, 128], F32)
        ones1 = sbt("ones1s", [1, 128], F32)
        ones128 = sbt("ones128s", [128, 1], F32)
        asc8 = abuf[:, 0:2048].bitcast(F32)
        a8 = abuf[:, 2048:3072]
        at_sb = abuf[:, 3072:3328]
        vt_sb = abuf[:, 3328:3584]

        PS = [es.enter_context(nc.psum_tensor(f"ps{i}", [128, 512], F32)) for i in range(8)]
        sems = {s: es.enter_context(nc.semaphore(s)) for s in
                ("s_x", "s_wa", "s_wb", "s_gp", "s_gp2", "s_pe", "s_ve", "s_se", "s_out")}
        block = es.enter_context(nc.Block())

        psum_free = {i: ("s_ve", 0) for i in range(8)}
        state = {"lntmp": ("s_ve", 0)}
        lntmp_f = warn[:, 12288:14336]
        scr_f = warn[:, 14336:16384].bitcast(F32)
        lntmp_a = xln[:, 12288:14336]
        scr_a = xln[:, 14336:16384].bitcast(F32)

        for dst, src in ((bqkv, d_bqkv), (cst, d_cst), (ident, d_ident),
                         (maskB, d_mask), (ones1, d_ones1), (ones128, d_ones128),
                         (bo_row, d_bo), (ones1b, d_ones1b)):
            OP("sync", lambda e, d=dst, s=src: e.dma_start(out=d[:, :], in_=s[:, :]),
               inc="s_x", inc_by=16)
        preload_done = p.cur("s_x")

        slabs = [oarn[:, 0:4096].bitcast(F32), oarn[:, 4096:8192].bitcast(F32)]
        consumed = {}
        ci = 0
        for wi in range(3):
            for j in range(43):
                par = ci % 2
                psem = "s_wa" if par == 0 else "s_wb"
                waits = [consumed[par]] if par in consumed else []
                OP("sync", lambda e, s=slabs[par], src=fws[wi][j]: e.dma_start(out=s, in_=src),
                   waits=waits, inc=psem, inc_by=16)
                ld = p.cur(psem)
                OP("vector", lambda e, d=prt[:, wi * 48 + j: wi * 48 + j + 1], s=slabs[par]:
                   e.tensor_reduce(d, s, AX.X, ALU.add, apply_absolute_value=True),
                   waits=[(psem, ld)], inc="s_ve")
                consumed[par] = ("s_ve", p.cur("s_ve"))
                ci += 1
        OP("vector", lambda e: e.drain())
        for wi in range(3):
            OP("vector", lambda e, w=wi: e.tensor_reduce(
                pr1[:, w:w + 1], prt[:, w * 48:w * 48 + 43], AX.X, ALU.add),
               inc="s_ve" if wi == 2 else None)
        ve_pr = p.cur("s_ve")
        OP("tensor", lambda e: e.matmul(PS[0][0:1, 0:3], ones128[:, 0:1], pr1[:, 0:3],
                                        start=True, stop=True),
           waits=[("s_ve", ve_pr), ("s_x", preload_done)], inc="s_pe")
        pe_s = p.cur("s_pe")
        OP("vector", lambda e: e.tensor_scalar(sv3[0:1, 0:3], PS[0][0:1, 0:3],
                                               1.0 / CNT_REAL, None, ALU.mult),
           waits=[("s_pe", pe_s)], inc="s_ve")
        psum_free[0] = ("s_ve", p.cur("s_ve"))
        der = sv3[0:1, 8:16]
        OP("vector", lambda e: e.drain())
        OP("vector", lambda e: e.tensor_copy(der[:, 0:1], sv3[0:1, 0:1]))
        OP("vector", lambda e: e.tensor_tensor(der[:, 1:2], sv3[0:1, 2:3], sv3[0:1, 1:2], ALU.mult))
        specs = [(0, 0.5), (0, -0.5), (1, 0.5), (1, -0.5), (2, 0.5), (2, -0.5)]
        for k, (src_i, sc) in enumerate(specs):
            OP("vector", lambda e, k=k, si=src_i, sc=sc: e.tensor_scalar(
                der[:, 2 + k:3 + k], sv3[0:1, si:si + 1], sc, None, ALU.mult))
        OP("vector", lambda e: e.drain())
        OP("vector", lambda e: e.tensor_scalar(der[:, 1:2], der[:, 1:2], 0.5, None, ALU.mult))
        OP("vector", lambda e: e.drain())
        OP("vector", lambda e: e.nop(), inc="s_ve")
        ve_der = p.cur("s_ve")
        OP("tensor", lambda e: e.matmul(PS[1][:, 0:8], ones1[0:1, :], der,
                                        start=True, stop=True),
           waits=[("s_ve", ve_der)], inc="s_pe")
        pe_b = p.cur("s_pe")
        OP("vector", lambda e: e.tensor_copy(sbc[:], PS[1][:, 0:8]),
           waits=[("s_pe", pe_b)], inc="s_ve")
        ve_sbc = p.cur("s_ve")
        psum_free[1] = ("s_ve", ve_sbc)

        S1V = sbc[:, 0:1]
        GAM = sbc[:, 1:2]
        THR = {("w1", 0): sbc[:, 2:3], ("w1", 1): sbc[:, 3:4],
               ("w3", 0): sbc[:, 4:5], ("w3", 1): sbc[:, 5:6],
               ("w2", 0): sbc[:, 6:7], ("w2", 1): sbc[:, 7:8]}

        def emit_ln(ntt, tcw, extra_waits, lntmp, scr):
            nm = stats[:, 0:ntt]
            sqa = stats[:, 16:16 + ntt]
            sqb = stats[:, 24:24 + ntt]
            rstd = stats[:, 32:32 + ntt]
            nmr = stats[:, 40:40 + ntt]
            scr2 = lntmp[:, 0:2048].bitcast(F32)
            xlnv = xln[:, 0:DK * tcw].rearrange("p (c t) -> p c t", c=DK)
            for tt in range(ntt):
                xt = xch[:, tt * D:(tt + 1) * D]
                OP("vector", lambda e, d=nm[:, tt:tt + 1], s=xt: e.tensor_reduce(
                    d, s, AX.X, ALU.add, negate=True),
                   waits=extra_waits if tt == 0 else [], inc="s_ve")
                if tt == 0:
                    sq_w = extra_waits + [state.get("scr", ("s_ve", 0)), state["lntmp"]]
                else:
                    sq_w = [("s_se", p.cur("s_se"))]
                OP("scalar", lambda e, d=sqa[:, tt:tt + 1], s=xt: e.activation(
                    scr[:, 0:1024], s[:, 0:1024], AF.Square, accum_out=d), waits=sq_w)
                OP("scalar", lambda e, d=sqb[:, tt:tt + 1], s=xt: e.activation(
                    scr2[:, 0:1024], s[:, 1024:2048], AF.Square, accum_out=d), inc="s_se")
            OP("scalar", lambda e: e.drain())
            OP("scalar", lambda e: e.nop(), inc="s_se")
            OP("vector", lambda e: e.drain())
            ve0, se0 = p.cur("s_ve"), p.cur("s_se")
            OP("vector", lambda e: e.tensor_scalar(nm, nm, 1.0 / D, None, ALU.mult),
               waits=[("s_se", se0)])
            OP("vector", lambda e: e.tensor_tensor(sqa, sqa, sqb, ALU.add))
            OP("vector", lambda e: e.drain())
            OP("vector", lambda e: e.tensor_tensor(sqb, nm, nm, ALU.mult))
            OP("vector", lambda e: e.tensor_scalar(sqa, sqa, 1.0 / D, None, ALU.mult))
            OP("vector", lambda e: e.drain())
            OP("vector", lambda e: e.tensor_tensor(sqa, sqa, sqb, ALU.subtract))
            OP("vector", lambda e: e.drain())
            OP("vector", lambda e: e.nop(), inc="s_ve")
            ve1 = p.cur("s_ve")
            OP("scalar", lambda e: e.activation(rstd, sqa, AF.Ln, bias=cst[:, 0:1]),
               waits=[("s_ve", ve1)], inc="s_se")
            OP("scalar", lambda e: e.activation(rstd, rstd, AF.Exp, scale=cst[:, 1:2]),
               waits=[("s_se", p.cur("s_se"))], inc="s_se")
            se1 = p.cur("s_se")
            OP("vector", lambda e: e.tensor_tensor(nmr, nm, rstd, ALU.mult),
               waits=[("s_se", se1)], inc="s_ve")
            ve2 = p.cur("s_ve")
            state["scr"] = ("s_se", se1)
            for tt in range(ntt):
                xt = xch[:, tt * D:(tt + 1) * D]
                OP("scalar", lambda e, s=xt, tt=tt: e.activation(
                    lntmp[:], s, AF.Identity, bias=nmr[:, tt:tt + 1], scale=rstd[:, tt:tt + 1]),
                   waits=[("s_ve", ve2), state["lntmp"]], inc="s_se")
                se_t = p.cur("s_se")
                for dk4 in range(4):
                    bi = 6 + (tt * 4 + dk4) % 2
                    bview = PS[bi][:, :].bitcast(BF16)
                    for j in range(4):
                        dk = dk4 * 4 + j
                        OP("tensor", lambda e, bv=bview, j=j, dk=dk: e.transpose(
                            bv[:, j * 128:(j + 1) * 128],
                            lntmp[:, dk * 128:(dk + 1) * 128], ident[:]),
                           waits=[("s_se", se_t), psum_free[bi]] if j == 0 else [],
                           inc="s_pe" if j == 3 else None)
                    pe_t = p.cur("s_pe")
                    dst = xlnv[:, dk4 * 4:dk4 * 4 + 4, tt * 128:(tt + 1) * 128]
                    OP("vector", lambda e, d=dst, bv=bview: e.tensor_copy(
                        d, bv[:, 0:512].rearrange("p (c t) -> p c t", c=4)),
                       waits=[("s_pe", pe_t)], inc="s_ve")
                    psum_free[bi] = ("s_ve", p.cur("s_ve"))
                state["lntmp"] = ("s_ve", p.cur("s_ve"))
            return ("s_ve", p.cur("s_ve"))

        def emit_ffn(src_view, dst_view, do_quant, entry_waits):
            q1slab = [warn[:, 0:2048], warn[:, 2048:4096]]
            q3slab = [warn[:, 4096:6144], warn[:, 6144:8192]]
            gseg = big[:, 0:11264]
            q2h = big[:, 11264:22528]
            f32slab = [oarn[:, 0:4096].bitcast(F32), oarn[:, 4096:8192].bitcast(F32)]
            acmp = [abuf[:, 0:2048], abuf[:, 2048:4096],
                    warn[:, 8192:10240], warn[:, 10240:12288]]
            sf = dict(state.get("ffn_sf", {}))

            for tc in range(2):
                OP("sync", lambda e, s=src_view[:, tc * FTT:(tc + 1) * FTT, :]: e.dma_start(
                    out=xch[:, :].rearrange("p (t d) -> p t d", t=FTT), in_=s),
                   waits=[("s_out", p.cur("s_out"))] + (entry_waits if tc == 0 else []),
                   inc="s_x", inc_by=16)
                xld = p.cur("s_x")
                xln_ready = emit_ln(FTT, FTC, [("s_x", xld)], lntmp_f, scr_f)

                hc0 = 0
                for seg, nq in enumerate(SEGS):
                    for hq in range(nq):
                        hc = hc0 + hq
                        par = hc % 2
                        pb = [(hc % 2) * 4 + i for i in range(4)]
                        if do_quant:
                            OP("sync", lambda e, src=v_w1[:, :, hc * 128:(hc + 1) * 128], d=f32slab[0]:
                               e.dma_start(out=d.rearrange("p (c f) -> p c f", c=DK), in_=src),
                               waits=[sf.get(("f32", 0), ("s_ve", 0))], inc="s_wa", inc_by=16)
                            ld1 = p.cur("s_wa")
                            OP("sync", lambda e, src=v_w3[:, :, hc * 128:(hc + 1) * 128], d=f32slab[1]:
                               e.dma_start(out=d.rearrange("p (c f) -> p c f", c=DK), in_=src),
                               waits=[sf.get(("f32", 1), ("s_ve", 0))], inc="s_wa", inc_by=16)
                            ld3 = p.cur("s_wa")
                            qw = [("s_wa", ld3), ("s_ve", ve_sbc),
                                  sf.get(("w1q", par), ("s_pe", 0)),
                                  sf.get(("w3q", par), ("s_pe", 0)),
                                  ("s_gp", p.cur("s_gp"))]
                            OP("vector", lambda e: e.drain())
                            OP("vector", lambda e: e.tensor_scalar(
                                acmp[0], f32slab[0], THR[("w1", 0)], None, ALU.is_ge), waits=qw)
                            OP("vector", lambda e: e.tensor_scalar(
                                acmp[1], f32slab[0], THR[("w1", 1)], None, ALU.is_le))
                            OP("vector", lambda e: e.tensor_scalar(
                                acmp[2], f32slab[1], THR[("w3", 0)], None, ALU.is_ge))
                            OP("vector", lambda e: e.tensor_scalar(
                                acmp[3], f32slab[1], THR[("w3", 1)], None, ALU.is_le))
                            OP("vector", lambda e: e.drain())
                            OP("vector", lambda e, q=q1slab[par]: e.tensor_tensor(
                                q, acmp[0], acmp[1], ALU.subtract))
                            OP("vector", lambda e, q=q3slab[par]: e.tensor_tensor(
                                q, acmp[2], acmp[3], ALU.subtract))
                            OP("vector", lambda e: e.drain())
                            OP("vector", lambda e: e.nop(), inc="s_ve")
                            qve = p.cur("s_ve")
                            sf[("f32", 0)] = ("s_ve", qve)
                            sf[("f32", 1)] = ("s_ve", qve)
                            OP("gpsimd", lambda e, q=q1slab[par], c=d_q1c[hc]: e.dma_start(out=c, in_=q),
                               waits=[("s_ve", qve)], inc="s_gp", inc_by=16)
                            sf[("w1c", par)] = ("s_gp", p.cur("s_gp"))
                            OP("gpsimd", lambda e, q=q3slab[par], c=d_q3c[hc]: e.dma_start(out=c, in_=q),
                               waits=[("s_ve", qve)], inc="s_gp", inc_by=16)
                            sf[("w3c", par)] = ("s_gp", p.cur("s_gp"))
                            qready = [("s_ve", qve)]
                            OP("sync", lambda e, src=d_w2[hc * 128:(hc + 1) * 128, :], d=f32slab[0]:
                               e.dma_start(out=d, in_=src),
                               waits=[("s_ve", qve)], inc="s_wb", inc_by=16)
                            ld2 = p.cur("s_wb")
                            q2t = q1slab[1 - par]
                            qw2 = [("s_wb", ld2),
                                   sf.get(("w1q", 1 - par), ("s_pe", 0)),
                                   ("s_gp", p.cur("s_gp")), ("s_gp2", p.cur("s_gp2"))]
                            OP("vector", lambda e: e.drain())
                            OP("vector", lambda e: e.tensor_scalar(
                                acmp[0], f32slab[0], THR[("w2", 0)], None, ALU.is_ge), waits=qw2)
                            OP("vector", lambda e: e.tensor_scalar(
                                acmp[1], f32slab[0], THR[("w2", 1)], None, ALU.is_le))
                            OP("vector", lambda e: e.drain())
                            OP("vector", lambda e, q=q2t: e.tensor_tensor(
                                q, acmp[0], acmp[1], ALU.subtract))
                            OP("vector", lambda e: e.drain())
                            OP("vector", lambda e: e.nop(), inc="s_ve")
                            qve2 = p.cur("s_ve")
                            sf[("f32", 0)] = ("s_ve", qve2)
                            OP("gpsimd", lambda e, q=q2t, c=d_q2c[hc]: e.dma_start(out=c, in_=q),
                               waits=[("s_ve", qve2)], inc="s_gp2", inc_by=16)
                            sf[("w1q", 1 - par)] = ("s_gp2", p.cur("s_gp2"))
                            sf[("w1c", 1 - par)] = ("s_gp2", p.cur("s_gp2"))
                        else:
                            psem = "s_wa" if par == 0 else "s_wb"
                            OP("sync", lambda e, d=q1slab[par], c=d_q1c[hc]: e.dma_start(out=d, in_=c),
                               waits=[sf.get(("w1q", par), ("s_pe", 0)),
                                      state.get("cache", ("s_gp", 0)),
                                      state.get("cache2", ("s_gp2", 0))],
                               inc=psem, inc_by=16)
                            OP("sync", lambda e, d=q3slab[par], c=d_q3c[hc]: e.dma_start(out=d, in_=c),
                               waits=[sf.get(("w3q", par), ("s_pe", 0))], inc=psem, inc_by=16)
                            qready = [(psem, p.cur(psem))]
                        mmw = qready + [xln_ready] + [psum_free[b] for b in pb]
                        for dk in range(DK):
                            for wsl, b0 in ((q1slab[par], 0), (q3slab[par], 2)):
                                for half in range(2):
                                    OP("tensor", lambda e, l=wsl[:, dk * 128:(dk + 1) * 128],
                                       r=xln[:, dk * FTC + half * 512: dk * FTC + (half + 1) * 512],
                                       o=PS[pb[b0 + half]][:, :], s=(dk == 0), st=(dk == DK - 1):
                                       e.matmul(o, l, r, start=s, stop=st),
                                       waits=mmw if (dk == 0 and b0 == 0 and half == 0) else [],
                                       inc="s_pe" if (dk == DK - 1 and b0 == 2 and half == 1) else None)
                        pe_h = p.cur("s_pe")
                        sf[("w1q", par)] = ("s_pe", pe_h)
                        sf[("w3q", par)] = ("s_pe", pe_h)
                        for half in range(2):
                            OP("scalar", lambda e, src=PS[pb[half]][:, :],
                               d=scr_f[:, half * 512:(half + 1) * 512]:
                               e.activation(d, src, AF.Silu, scale=S1V),
                               waits=[("s_pe", pe_h), sf.get(("scr", half), ("s_ve", 0)),
                                      state.get("scr", ("s_se", 0))], inc="s_se")
                            se_h = p.cur("s_se")
                            OP("vector", lambda e, d=gseg[:, hq * FTC + half * 512: hq * FTC + (half + 1) * 512],
                               sl=scr_f[:, half * 512:(half + 1) * 512], h3=PS[pb[2 + half]][:, :]:
                               e.scalar_tensor_tensor(d, sl, GAM, h3, ALU.mult, ALU.mult),
                               waits=[("s_se", se_h)], inc="s_ve")
                            sf[("scr", half)] = ("s_ve", p.cur("s_ve"))
                        ve_g = p.cur("s_ve")
                        for b in pb:
                            psum_free[b] = ("s_ve", ve_g)
                    state["scr"] = ("s_ve", 0)
                    for dh in range(2):
                        for hq in range(nq):
                            hc = hc0 + hq
                            OP("sync", lambda e, d=q2h[:, hq * 1024:(hq + 1) * 1024],
                               c=d_q2c[hc][:, dh * 1024:(dh + 1) * 1024]:
                               e.dma_start(out=d, in_=c),
                               waits=([sf.get("q2h", ("s_pe", 0))] +
                                      ([("s_gp2", p.cur("s_gp2"))] if do_quant else
                                       [state.get("cache2", ("s_gp2", 0))]))
                               if hq == 0 else [],
                               inc="s_wa", inc_by=16)
                        q2ld = p.cur("s_wa")
                        for tt in range(FTT):
                            for dc in range(2):
                                b = (tt % 4) * 2 + dc
                                w = [("s_wa", q2ld), ("s_ve", ve_g), psum_free[b]]
                                for hq in range(nq):
                                    OP("tensor", lambda e,
                                       l=gseg[:, hq * FTC + tt * 128: hq * FTC + tt * 128 + 128],
                                       r=q2h[:, hq * 1024 + dc * 512: hq * 1024 + (dc + 1) * 512],
                                       o=PS[b][:, :], s=(hq == 0), st=(hq == nq - 1):
                                       e.matmul(o, l, r, start=s, stop=st),
                                       waits=w if hq == 0 else [],
                                       inc="s_pe" if hq == nq - 1 else None)
                                pe_y = p.cur("s_pe")
                                OP("vector", lambda e,
                                   xp=xch[:, tt * D + (dh * 2 + dc) * 512: tt * D + (dh * 2 + dc + 1) * 512],
                                   ps_=PS[b][:, :]: e.tensor_tensor(xp, xp, ps_, ALU.add),
                                   waits=[("s_pe", pe_y)], inc="s_ve")
                                psum_free[b] = ("s_ve", p.cur("s_ve"))
                        sf["q2h"] = ("s_pe", p.cur("s_pe"))
                    hc0 += nq
                ve_all = p.cur("s_ve")
                OP("sync", lambda e, d=dst_view[:, tc * FTT:(tc + 1) * FTT, :]: e.dma_start(
                    out=d, in_=xch[:, :].rearrange("p (t d) -> p t d", t=FTT)),
                   waits=[("s_ve", ve_all)], inc="s_out", inc_by=16)
                if do_quant and tc == 0:
                    state["cache"] = ("s_gp", p.cur("s_gp"))
                    state["cache2"] = ("s_gp2", p.cur("s_gp2"))
                    do_quant = False
            state["ffn_sf"] = {}

        def emit_attn_src(src_view, o_view, entry_waits):
            qpk = big[:, 0:8192]
            kpk = big[:, 8192:16384]
            vpk = big[:, 16384:24576]
            packs = {0: qpk, 1: kpk, 2: vpk}
            pvw = {0: v_wq, 1: v_wk, 2: v_wv}
            wslab = [xln[:, DK * ATC:DK * ATC + 2048], xln[:, DK * ATC + 2048:DK * ATC + 4096]]
            sf = {}
            for tc in range(4):
                OP("sync", lambda e, s=src_view[:, tc * ATT:(tc + 1) * ATT, :]: e.dma_start(
                    out=xch[:, 0:ATT * D].rearrange("p (t d) -> p t d", t=ATT), in_=s),
                   waits=[("s_out", p.cur("s_out"))] + (entry_waits if tc == 0 else []),
                   inc="s_x", inc_by=16)
                xld = p.cur("s_x")
                xln_ready = emit_ln(ATT, ATC, [("s_x", xld)], lntmp_a, scr_a)
                for pi in range(3):
                    for f in range(H):
                        par = (pi * H + f) % 2
                        psem = "s_gp" if par == 0 else "s_gp2"
                        OP("gpsimd", lambda e, src=pvw[pi][:, :, f * 128:(f + 1) * 128], d=wslab[par]:
                           e.dma_start(out=d.rearrange("p (c f) -> p c f", c=DK), in_=src),
                           waits=[sf.get(("pw", par), ("s_pe", 0))] + (entry_waits if (tc == 0 and pi == 0 and f < 2) else []),
                           inc=psem, inc_by=16)
                        ldw = p.cur(psem)
                        bi = (pi * H + f) % 4
                        mw = [(psem, ldw), xln_ready, psum_free[bi]]
                        for dk in range(DK):
                            OP("tensor", lambda e, l=wslab[par][:, dk * 128:(dk + 1) * 128],
                               r=xln[:, dk * ATC:dk * ATC + ATC],
                               o=PS[bi][:, :], s=(dk == 0), st=(dk == DK - 1):
                               e.matmul(o, l, r, start=s, stop=st),
                               waits=mw if dk == 0 else [], inc="s_pe" if dk == DK - 1 else None)
                        pe_p = p.cur("s_pe")
                        sf[("pw", par)] = ("s_pe", pe_p)
                        dstv = packs[pi][:, :].rearrange("p (g f t) -> p g f t", g=NG, f=H)[:, :, f, :]
                        srcv = PS[bi][:, :].rearrange("p (g t) -> p g t", g=NG)
                        OP("scalar", lambda e, d=dstv, s=srcv,
                           b=bqkv[:, pi * 16 + f: pi * 16 + f + 1]:
                           e.activation(d, s, AF.Identity, bias=b),
                           waits=[("s_pe", pe_p)], inc="s_se")
                        psum_free[bi] = ("s_se", p.cur("s_se"))
                se_packs = p.cur("s_se")
                mx8 = stats[:, 48:56]
                sm8 = stats[:, 56:64]
                for b in range(NG // 8):
                    OP("vector", lambda e: e.drain())
                    for i in range(8):
                        g = b * 8 + i
                        par = g % 2
                        gsl = slice(g * 128, (g + 1) * 128)
                        OP("tensor", lambda e, l=qpk[:, gsl], r=kpk[:, gsl], o=PS[par][:, 0:128]:
                           e.matmul(o, l, r, start=True, stop=True),
                           waits=[("s_se", se_packs), psum_free[par]], inc="s_pe")
                        pe_g = p.cur("s_pe")
                        w = [("s_pe", pe_g)]
                        if b > 0 and i == 0:
                            w.append(state["a8free"])
                        OP("vector", lambda e, i=i, s=PS[par][:, 0:128]: e.scalar_tensor_tensor(
                            asc8[:, i * 128:(i + 1) * 128], s, cst[:, 2:3], maskB[:],
                            ALU.mult, ALU.add), waits=w, inc="s_ve")
                        psum_free[par] = ("s_ve", p.cur("s_ve"))
                    OP("vector", lambda e: e.drain())
                    for i in range(8):
                        OP("vector", lambda e, i=i: e.tensor_reduce(
                            mx8[:, i:i + 1], asc8[:, i * 128:(i + 1) * 128], AX.X, ALU.max,
                            negate=True), inc="s_ve" if i == 7 else None)
                    ve_rm = p.cur("s_ve")
                    for i in range(8):
                        OP("scalar", lambda e, i=i: e.activation(
                            asc8[:, i * 128:(i + 1) * 128], asc8[:, i * 128:(i + 1) * 128],
                            AF.Exp, bias=mx8[:, i:i + 1]),
                           waits=[("s_ve", ve_rm)] if i == 0 else [],
                           inc="s_se" if i == 7 else None)
                    se_exp = p.cur("s_se")
                    for i in range(8):
                        OP("vector", lambda e, i=i: e.tensor_reduce(
                            sm8[:, i:i + 1], asc8[:, i * 128:(i + 1) * 128], AX.X, ALU.add),
                           waits=[("s_se", se_exp)] if i == 0 else [])
                    OP("vector", lambda e: e.drain())
                    OP("vector", lambda e: e.reciprocal(sm8, sm8))
                    OP("vector", lambda e: e.drain())
                    for i in range(8):
                        w = []
                        if i == 0 and "at_free" in state:
                            w = [state["at_free"]]
                        OP("vector", lambda e, i=i: e.tensor_scalar(
                            a8[:, i * 128:(i + 1) * 128], asc8[:, i * 128:(i + 1) * 128],
                            sm8[:, i:i + 1], None, ALU.mult),
                           waits=w, inc="s_ve" if i == 7 else None)
                    ve_norm = p.cur("s_ve")
                    for i in range(8):
                        g = b * 8 + i
                        par = g % 2
                        gsl = slice(g * 128, (g + 1) * 128)
                        OP("tensor", lambda e, o=PS[4 + par][:, :].bitcast(BF16)[:, 0:128],
                           i_=a8[:, i * 128:(i + 1) * 128]: e.transpose(o, i_, ident[:]),
                           waits=[("s_ve", ve_norm), psum_free[4 + par]], inc="s_pe")
                        pe_at = p.cur("s_pe")
                        OP("tensor", lambda e, o=PS[6 + par][:, :].bitcast(BF16)[:, 0:128],
                           i_=vpk[:, gsl]: e.transpose(o, i_, ident[:]),
                           waits=[psum_free[6 + par]], inc="s_pe")
                        pe_vt = p.cur("s_pe")
                        OP("vector", lambda e, d=at_sb[:, par * 128:(par + 1) * 128],
                           s=PS[4 + par][:, :].bitcast(BF16)[:, 0:128]: e.tensor_copy(d, s),
                           waits=[("s_pe", pe_at)], inc="s_ve")
                        psum_free[4 + par] = ("s_ve", p.cur("s_ve"))
                        OP("vector", lambda e, d=vt_sb[:, par * 128:(par + 1) * 128],
                           s=PS[6 + par][:, :].bitcast(BF16)[:, 0:128]: e.tensor_copy(d, s),
                           waits=[("s_pe", pe_vt)], inc="s_ve")
                        ve_t = p.cur("s_ve")
                        psum_free[6 + par] = ("s_ve", ve_t)
                        OP("tensor", lambda e, l=vt_sb[:, par * 128:(par + 1) * 128],
                           r=at_sb[:, par * 128:(par + 1) * 128], o=PS[2 + par][:, 0:128]:
                           e.matmul(o, l, r, start=True, stop=True),
                           waits=[("s_ve", ve_t), psum_free[2 + par]], inc="s_pe")
                        pe_av = p.cur("s_pe")
                        if i == 7:
                            state["a8free"] = ("s_pe", pe_av)
                            state["at_free"] = ("s_pe", pe_av)
                        dsto = oarn[:, :].rearrange("p (f t) -> p f t", f=H)[:, :, g * 8:(g + 1) * 8]
                        OP("scalar", lambda e, d=dsto,
                           s=PS[2 + par][:, 0:128].rearrange("p (f t) -> p f t", f=H):
                           e.activation(d, s, AF.Identity),
                           waits=[("s_pe", pe_av)], inc="s_se")
                        psum_free[2 + par] = ("s_se", p.cur("s_se"))
                se_mid = p.cur("s_se")
                OP("sync", lambda e, d=o_view[:, :, tc * ATC:(tc + 1) * ATC],
                   s=oarn[:, :].rearrange("p (f t) -> p f t", f=H):
                   e.dma_start(out=d, in_=s),
                   waits=[("s_se", se_mid)], inc="s_out", inc_by=16)
                state["o_done"] = ("s_out", p.cur("s_out"))

        def emit_attn_out(x1_view, dst_view, o_view, entry_waits):
            obuf = big[:, 0:8192]
            zbuf = xln[:, 0:8192]
            woslab = [warn[:, 0:8192], warn[:, 8192:16384]]
            sf = {}
            for otc in range(4):
                OP("sync", lambda e, s=x1_view[:, otc * ATT:(otc + 1) * ATT, :]: e.dma_start(
                    out=xch[:, 0:ATT * D].rearrange("p (t d) -> p t d", t=ATT), in_=s),
                   waits=[("s_out", p.cur("s_out"))] + (entry_waits if otc == 0 else []),
                   inc="s_x", inc_by=16)
                xld = p.cur("s_x")
                ow = [sf.get("obuf", ("s_ve", 0))] + (entry_waits if otc == 0 else [])
                OP("sync", lambda e, d=obuf.rearrange("p (h s) -> p h s", h=2),
                   s=o_view[:, 2 * otc:2 * otc + 2, :]: e.dma_start(out=d, in_=s),
                   waits=ow, inc="s_x", inc_by=16)
                old = p.cur("s_x")
                ovw = obuf.rearrange("p (h s) -> p h s", h=2)
                zvw = zbuf.rearrange("p (j s) -> p j s", j=16)
                for h2 in range(2):
                    for j in range(16):
                        w = [("s_x", old)] if (h2 == 0 and j == 0) else []
                        if h2 == 0 and j == 0 and "zbuf" in sf:
                            w.append(sf["zbuf"])
                        OP("vector", lambda e, h2=h2, j=j: e.tensor_copy(
                            zvw[:, j, h2 * 256:(h2 + 1) * 256],
                            ovw[:, h2, j:4096:16]),
                           waits=w, inc="s_ve" if (h2 == 1 and j == 15) else None)
                ve_z = p.cur("s_ve")
                for dc in range(4):
                    par = dc % 2
                    psem = "s_gp" if par == 0 else "s_gp2"
                    OP("gpsimd", lambda e, src=v_wo[:, :, dc * 512:(dc + 1) * 512], d=woslab[par]:
                       e.dma_start(out=d.rearrange("p (c f) -> p c f", c=DK), in_=src),
                       waits=[sf.get(("wo", par), ("s_pe", 0))] + (entry_waits if (otc == 0 and dc < 2) else []),
                       inc=psem, inc_by=16)
                    ldw = p.cur(psem)
                    for tt in range(ATT):
                        bi = 4 + (dc * ATT + tt) % 2
                        w = [(psem, ldw), ("s_ve", ve_z), psum_free[bi]]
                        for jj in range(H):
                            OP("tensor", lambda e,
                               l=zbuf[:, jj * ATC + tt * 128: jj * ATC + tt * 128 + 128],
                               r=woslab[par][:, jj * 512:(jj + 1) * 512],
                               o=PS[bi][:, :], s=(jj == 0):
                               e.matmul(o, l, r, start=s, stop=False),
                               waits=w if jj == 0 else [])
                        OP("tensor", lambda e, o=PS[bi][:, :],
                           r=bo_row[0:1, dc * 512:(dc + 1) * 512]:
                           e.matmul(o, ones1b[0:1, :], r, start=False, stop=True),
                           inc="s_pe")
                        pe_w = p.cur("s_pe")
                        if tt == ATT - 1:
                            sf[("wo", par)] = ("s_pe", pe_w)
                        OP("vector", lambda e,
                           xp=xch[:, tt * D + dc * 512: tt * D + (dc + 1) * 512],
                           ps_=PS[bi][:, :]: e.tensor_tensor(xp, xp, ps_, ALU.add),
                           waits=[("s_pe", pe_w), ("s_x", xld)], inc="s_ve")
                        psum_free[bi] = ("s_ve", p.cur("s_ve"))
                sf["obuf"] = ("s_pe", p.cur("s_pe"))
                sf["zbuf"] = ("s_pe", p.cur("s_pe"))
                ve_all = p.cur("s_ve")
                OP("sync", lambda e, d=dst_view[:, otc * ATT:(otc + 1) * ATT, :]: e.dma_start(
                    out=d, in_=xch[:, 0:ATT * D].rearrange("p (t d) -> p t d", t=ATT)),
                   waits=[("s_ve", ve_all)], inc="s_out", inc_by=16)

        if mode == "p1":
            entry = [("s_pe", p.cur("s_pe")), ("s_ve", p.cur("s_ve")), ("s_se", p.cur("s_se"))]
            emit_ffn(v_x, d_x1o.rearrange("(t p) d -> p t d", p=128), do_quant=True,
                     entry_waits=entry)
            entry = [("s_pe", p.cur("s_pe")), ("s_ve", p.cur("s_ve")), ("s_se", p.cur("s_se"))]
            emit_attn_src(d_x1o.rearrange("(t p) d -> p t d", p=128),
                          d_oo.rearrange("h d s -> d h s"), entry)
        else:
            entry = [("s_pe", p.cur("s_pe")), ("s_ve", p.cur("s_ve")), ("s_se", p.cur("s_se"))]
            emit_attn_out(d_x1i.rearrange("(t p) d -> p t d", p=128), v_xm2,
                          d_oi.rearrange("h d s -> d h s"), entry)
            entry = [("s_pe", p.cur("s_pe")), ("s_ve", p.cur("s_ve")), ("s_se", p.cur("s_se"))]
            emit_ffn(v_xm2, v_y, do_quant=True, entry_waits=entry)

        OP("sync", lambda e: e.nop(), waits=[("s_out", p.cur("s_out")),
                                             ("s_gp", p.cur("s_gp")),
                                             ("s_gp2", p.cur("s_gp2")),
                                             ("s_wa", p.cur("s_wa")),
                                             ("s_wb", p.cur("s_wb"))])

        @block.sync
        def _(eng):
            p.emit(eng, "sync", sems)

        @block.gpsimd
        def _(eng):
            p.emit(eng, "gpsimd", sems)

        @block.tensor
        def _(eng):
            p.emit(eng, "tensor", sems)

        @block.vector
        def _(eng):
            p.emit(eng, "vector", sems)

        @block.scalar
        def _(eng):
            p.emit(eng, "scalar", sems)

    return nc


def _host_prep(inputs):
    x = np.ascontiguousarray(np.asarray(inputs["L0_attn_x"], dtype=np.float32).reshape(NTOK * NCORES, D))
    wqT = np.ascontiguousarray(np.asarray(inputs["L0_attn_Wq"], dtype=np.float32).T)
    wkT = np.ascontiguousarray(np.asarray(inputs["L0_attn_Wk"], dtype=np.float32).T)
    wvT = np.ascontiguousarray(np.asarray(inputs["L0_attn_Wv"], dtype=np.float32).T)
    woT = np.ascontiguousarray(np.asarray(inputs["L0_attn_Wo"], dtype=np.float32).T)
    w1T = np.zeros((D, HID), dtype=np.float32)
    w1T[:, :HID_REAL] = np.asarray(inputs["L0_ffn_w1"], dtype=np.float32).T
    w3T = np.zeros((D, HID), dtype=np.float32)
    w3T[:, :HID_REAL] = np.asarray(inputs["L0_ffn_w3"], dtype=np.float32).T
    w2T = np.zeros((HID, D), dtype=np.float32)
    w2T[:HID_REAL, :] = np.asarray(inputs["L0_ffn_w2"], dtype=np.float32).T
    bq = np.asarray(inputs["L0_attn_bq"], dtype=np.float32)
    bk = np.asarray(inputs["L0_attn_bk"], dtype=np.float32)
    bv = np.asarray(inputs["L0_attn_bv"], dtype=np.float32)
    bo = np.asarray(inputs["L0_attn_bo"], dtype=np.float32)
    bqkv = np.zeros((128, 48), dtype=np.float32)
    for pi, b in enumerate((bq, bk, bv)):
        bqkv[:, pi * 16:(pi + 1) * 16] = b.reshape(16, 128).T
    bo_row = bo[None, :].astype(ml_dtypes.bfloat16)
    ones1b = np.ones((1, 128), dtype=np.float32).astype(ml_dtypes.bfloat16)
    ident = np.eye(128, dtype=np.float32).astype(ml_dtypes.bfloat16)
    m = np.arange(128)
    maskB = np.where((m[:, None] % 8) == (m[None, :] % 8), 0.0, -1e9).astype(np.float32)
    cst = np.tile(np.array([[EPS, -0.5, 1.0 / np.sqrt(128.0), 0.0]], dtype=np.float32), (128, 1))
    ones1 = np.ones((1, 128), dtype=np.float32)
    ones128 = np.ones((128, 1), dtype=np.float32)
    shared = dict(wqT=wqT, wkT=wkT, wvT=wvT, woT=woT, w1T=w1T, w3T=w3T, w2T=w2T,
                  bqkv=bqkv, bo_row=bo_row, ones1b=ones1b, ident=ident, maskB=maskB,
                  cst=cst, ones1=ones1, ones128=ones128)
    in_maps = []
    for c in range(NCORES):
        m_ = dict(shared)
        m_["x_in"] = np.ascontiguousarray(x[c * NTOK:(c + 1) * NTOK])
        in_maps.append(m_)
    return in_maps


def kernel(**inputs):
    in_maps = _host_prep(inputs)
    nc1 = build_nc("p1")
    res1 = run_bass_kernel_spmd(nc1, in_maps, core_ids=list(range(NCORES)))
    o_out = [np.asarray(r["o_out"]) for r in res1.results]
    x1_out = [np.asarray(r["x1_out"]) for r in res1.results]
    in_maps2 = []
    for c in range(NCORES):
        m_ = dict(in_maps[c])
        c0 = (c // 2) * 2
        hs = slice(8 * (c % 2), 8 * (c % 2) + 8)
        m_["x1_in"] = x1_out[c]
        m_["o_in"] = np.ascontiguousarray(
            np.concatenate([o_out[c0][hs], o_out[c0 + 1][hs]], axis=2))
        in_maps2.append(m_)
    nc2 = build_nc("p2")
    res2 = run_bass_kernel_spmd(nc2, in_maps2, core_ids=list(range(NCORES)))
    y = np.concatenate([np.asarray(r["y_out"]) for r in res2.results], axis=0)
    return np.ascontiguousarray(y.reshape(4, 4096, D)).astype(np.float32)
